# version 03
# baseline (speedup 1.0000x reference)
import numpy as np
import ml_dtypes
from contextlib import ExitStack

import concourse.bass as bass
import concourse.mybir as mybir
from concourse.bass_utils import run_bass_kernel_spmd

F32 = mybir.dt.float32
BF16 = mybir.dt.bfloat16
I32 = mybir.dt.int32
AF = mybir.ActivationFunctionType
ALU = mybir.AluOpType
AX = mybir.AxisListType
BF = ml_dtypes.bfloat16

D = 1024
T = 2048
NCH = D // 128
f32 = np.float32

ALL_SUBLAYERS = [
    "l0f", "l0a", "l0f2",
    "l1f", "l1a", "l1f2",
    "l2f", "l2a", "l2f2",
    "l3f", "l3a", "l3f2",
]


class Prog:
    ENGS = ("pe", "act", "dve", "pool", "sp")

    def __init__(self, nc):
        self.nc = nc
        self.ops = {e: [] for e in self.ENGS}
        self.last_write = {}
        self.readers = {}
        self.gidx = 0

    def emit(self, eng, method, *args, reads=(), writes=(), dma=False, **kwargs):
        kind = "d" if dma else "c"
        idx = len(self.ops[eng])
        deps = {}

        def add_dep(e2, i2, k2):
            if e2 == eng and k2 == kind and k2 == "c" and eng == "pe":
                return
            deps[(e2, k2, i2)] = True

        for k in reads:
            lw = self.last_write.get(k)
            if lw is not None:
                add_dep(*lw)
        for k in writes:
            lw = self.last_write.get(k)
            if lw is not None:
                add_dep(*lw)
            for (e2, k2), i2 in self.readers.get(k, {}).items():
                if not (e2 == eng and k2 == kind and i2 == idx):
                    add_dep(e2, i2, k2)

        self.ops[eng].append(dict(method=method, args=args, kwargs=kwargs,
                                  kind=kind, deps=deps, inc=False, gidx=self.gidx,
                                  skey=(writes[0] if (dma and writes) else None)))
        self.gidx += 1
        for k in reads:
            self.readers.setdefault(k, {})[(eng, kind)] = idx
        for k in writes:
            self.last_write[k] = (eng, idx, kind)
            self.readers[k] = {}
        return (eng, idx)

    def mark(self, ref):
        eng, idx = ref
        self.ops[eng][idx]["inc"] = True

    def finalize(self, es):
        nc = self.nc
        def src_of(e2, i2):
            rec = self.ops[e2][i2]
            if rec["kind"] == "d":
                return ("k", rec["skey"])
            return ("c", e2)

        for e in self.ENGS:
            for rec in self.ops[e]:
                if rec["kind"] == "d":
                    rec["inc"] = True
                for (e2, k2, i2) in rec["deps"]:
                    self.ops[e2][i2]["inc"] = True

        sem_srcs = set()
        for e in self.ENGS:
            for rec in self.ops[e]:
                if rec["inc"]:
                    sem_srcs.add(("c", e) if rec["kind"] == "c" else ("k", rec["skey"]))
        sems = {}
        for i, s in enumerate(sorted(sem_srcs)):
            sems[s] = es.enter_context(nc.semaphore(f"sem{i}"))
        assert len(sems) <= 90, f"too many semaphores: {len(sems)}"

        cums = {e: [0] * len(self.ops[e]) for e in self.ENGS}
        src_count = {}
        emit_order = []
        for e in self.ENGS:
            for i, rec in enumerate(self.ops[e]):
                emit_order.append((rec["gidx"], e, i))
        emit_order.sort()
        for _, e, i in emit_order:
            rec = self.ops[e][i]
            s = ("c", e) if rec["kind"] == "c" else ("k", rec["skey"])
            if rec["inc"]:
                src_count[s] = src_count.get(s, 0) + (16 if rec["kind"] == "d" else 1)
            cums[e][i] = src_count.get(s, 0)

        block = es.enter_context(nc.Block())
        ops = self.ops

        def run_engine(engname, engine):
            waited = {}
            for rec in ops[engname]:
                tgts = {}
                for (e2, k2, i2) in rec["deps"]:
                    s = src_of(e2, i2)
                    v = cums[e2][i2]
                    if v == 0:
                        continue
                    if tgts.get(s, 0) < v:
                        tgts[s] = v
                for s, v in sorted(tgts.items(), key=lambda x: str(x[0])):
                    if waited.get(s, 0) < v:
                        engine.wait_ge(sems[s], v)
                        waited[s] = v
                instr = getattr(engine, rec["method"])(*rec["args"], **rec["kwargs"])
                if rec["inc"]:
                    s = ("c", engname) if rec["kind"] == "c" else ("k", rec["skey"])
                    instr.then_inc(sems[s], 16 if rec["kind"] == "d" else 1)

        @block.sync
        def _(sync):
            run_engine("sp", sync)

        @block.vector
        def _(vector):
            run_engine("dve", vector)

        @block.scalar
        def _(scalar):
            run_engine("act", scalar)

        @block.tensor
        def _(tensor):
            run_engine("pe", tensor)

        @block.gpsimd
        def _(gpsimd):
            run_engine("pool", gpsimd)


class Core:
    def __init__(self):
        self.nc = bass.Bass("TRN2", target_bir_lowering=False)
        self.es = ExitStack()
        self.p = Prog(self.nc)
        self.in_arrays = {}
        self.cnt = 0

    def dram_in(self, name, arr):
        self.in_arrays[name] = arr
        dt = {np.dtype(np.float32): F32, np.dtype(BF): BF16,
              np.dtype(np.int32): I32}[arr.dtype]
        return self.nc.dram_tensor(name, list(arr.shape), dt,
                                   kind="ExternalInput").ap()

    def dram_out(self, name, shape, dtype=F32):
        return self.nc.dram_tensor(name, list(shape), dtype,
                                   kind="ExternalOutput").ap()

    def sb(self, shape, dtype, name=None):
        if name is None:
            name = f"t{self.cnt}"
            self.cnt += 1
        return self.es.enter_context(self.nc.sbuf_tensor(name, list(shape), dtype))

    def scratch(self, name, shape, dtype):
        if not hasattr(self, "_scr"):
            self._scr = {}
        kk = name
        if kk in self._scr:
            t, sh, dt_ = self._scr[kk]
            assert sh == list(shape) and dt_ == dtype, f"scratch {name} shape clash"
            return t
        t = self.sb(shape, dtype, name)
        self._scr[kk] = (t, list(shape), dtype)
        return t

    def ps(self, shape, name=None):
        if name is None:
            name = f"p{self.cnt}"
            self.cnt += 1
        return self.es.enter_context(self.nc.psum_tensor(name, list(shape), F32))


def K(t):
    return t.name



def dma(c, eng, dst_ap, src_ap, wkey, rkeys=()):
    return c.p.emit(eng, "dma_start", dst_ap, src_ap, dma=True,
                    reads=tuple(rkeys), writes=(wkey,))


def mm(c, out_ap, lhsT_ap, rhs_ap, rkeys, wkey, start, stop):
    return c.p.emit("pe", "matmul", out_ap, lhsT_ap, rhs_ap, start=start,
                    stop=stop, reads=tuple(rkeys), writes=(wkey,))


def trn(c, out_ap, in_ap, ident_ap, ident_key, rkeys, wkey):
    return c.p.emit("pe", "transpose", out_ap, in_ap, ident_ap,
                    reads=tuple(rkeys) + (ident_key,), writes=(wkey,))


def act(c, out_ap, in_ap, func, rkeys, wkey, **kw):
    return c.p.emit("act", "activation", out_ap, in_ap, func,
                    reads=tuple(rkeys), writes=(wkey,), **kw)


def tt(c, out_ap, a, b, op, rkeys, wkey, eng="dve"):
    return c.p.emit(eng, "tensor_tensor", out_ap, a, b, op,
                    reads=tuple(rkeys), writes=(wkey,))


def _sc(v):
    return float(v) if isinstance(v, (int, float, np.floating)) else v


def ts(c, out_ap, in_ap, s1, op, rkeys, wkey, eng="dve"):
    return c.p.emit(eng, "tensor_scalar", out_ap, in_ap, _sc(s1), None, op,
                    reads=tuple(rkeys), writes=(wkey,))


def stt(c, out_ap, in0, scalar, in1, op0, op1, rkeys, wkey, eng="dve"):
    return c.p.emit(eng, "scalar_tensor_tensor", out_ap, in0, _sc(scalar), in1,
                    op0, op1, reads=tuple(rkeys), writes=(wkey,))


def cp(c, out_ap, in_ap, rkeys, wkey, eng="dve"):
    return c.p.emit(eng, "tensor_copy", out_ap, in_ap,
                    reads=tuple(rkeys), writes=(wkey,))


def red(c, out_ap, in_ap, op, rkeys, wkey, negate=False):
    return c.p.emit("dve", "tensor_reduce", out_ap, in_ap, AX.X, op,
                    negate=negate, reads=tuple(rkeys), writes=(wkey,))


def recip(c, out_ap, in_ap, rkeys, wkey):
    return c.p.emit("dve", "reciprocal", out_ap, in_ap,
                    reads=tuple(rkeys), writes=(wkey,))


class Mod:
    def __init__(self, c: Core, dr):
        self.c = c
        self.dr = dr
        self.R = [c.sb([128, T], F32, f"R{i}") for i in range(NCH)]
        self.XL = [c.sb([128, T], BF16, f"XL{i}") for i in range(NCH)]
        self.PS = [c.ps([128, 512], f"PS{i}") for i in range(7)]
        self.PSB = c.es.enter_context(
            c.nc.psum_tensor("PSB", [128, 1024], BF16))
        self.ones_f = c.sb([128, 128], F32, "ones_f")
        self.ident_b = c.sb([128, 128], BF16, "ident_b")
        self.ident_f = c.sb([128, 128], F32, "ident_f")
        dma(c, "sp", self.ones_f[:], dr["c_ones_f"], K(self.ones_f))
        dma(c, "sp", self.ident_b[:], dr["c_ident_b"], K(self.ident_b))
        dma(c, "sp", self.ident_f[:], dr["c_ident_f"], K(self.ident_f))


def emit_load_x(c, m):
    for i in range(NCH):
        dma(c, "sp", m.R[i][:], m.dr["x_in"][128 * i:128 * (i + 1), :], K(m.R[i]))


def emit_store_x(c, m, out_ap):
    for i in range(NCH):
        dma(c, "sp", out_ap[128 * i:128 * (i + 1), :], m.R[i][:],
            "out_dram", rkeys=(K(m.R[i]),))
    c.p.emit("pool", "nop", reads=("out_dram",), writes=())


def emit_ln(c, m):
    sq = [c.scratch("sh_s0", [128, 512], F32) for i in range(2)]
    sumB, sqB = m.PS[0], m.PS[1]
    mean = c.scratch("sh_s1", [128, 512], F32)
    rstd = c.scratch("sh_s2", [128, 512], F32)
    for tb in range(4):
        sl = slice(512 * tb, 512 * (tb + 1))
        for i in range(NCH):
            mm(c, sumB[:], m.ones_f[:], m.R[i][:, sl],
               [K(m.ones_f), K(m.R[i])], K(sumB), i == 0, i == NCH - 1)
        for i in range(NCH):
            s = sq[i % 2]
            act(c, s[:], m.R[i][:, sl], AF.Square, [K(m.R[i])], K(s))
            mm(c, sqB[:], m.ones_f[:], s[:],
               [K(m.ones_f), K(s)], K(sqB), i == 0, i == NCH - 1)
        ts(c, mean[:], sumB[:], f32(1.0 / D), ALU.mult, [K(sumB)], K(mean))
        ts(c, rstd[:], sqB[:], f32(1.0 / D), ALU.mult, [K(sqB)], K(rstd))
        m2 = sq[0]
        act(c, m2[:], mean[:], AF.Square, [K(mean)], K(m2))
        tt(c, rstd[:], rstd[:], m2[:], ALU.subtract, [K(rstd), K(m2)], K(rstd))
        act(c, rstd[:], rstd[:], AF.Sqrt, [K(rstd)], K(rstd), bias=f32(1e-5))
        recip(c, rstd[:], rstd[:], [K(rstd)], K(rstd))
        for i in range(NCH):
            s = sq[i % 2]
            tt(c, s[:], m.R[i][:, sl], mean[:], ALU.subtract,
               [K(m.R[i]), K(mean)], K(s))
            tt(c, m.XL[i][:, sl], s[:], rstd[:], ALU.mult,
               [K(s), K(rstd)], K(m.XL[i]))



def emit_l0_ffn(c, m, pf="l0f"):
    E = 8
    dr = m.dr
    gst = c.scratch("l0_gst", [128, 8 * NCH], F32)
    gsb = c.scratch("l0_gsb", [128, 8 * NCH], BF16)
    dma(c, "sp", gst[:], dr["l0_gateT"], K(gst))
    cp(c, gsb[:], gst[:], [K(gst)], K(gsb), eng="pool")

    scP = m.PS[2]
    tpP = m.PS[3]
    sc = c.scratch("g_sc", [128, 8], F32)
    msk1 = c.scratch("g_msk1", [128, 8], F32)
    st1 = c.scratch("g_st1", [128, 1], F32)
    st2 = c.scratch("g_st2", [128, 1], F32)
    p1 = c.scratch("g_p1", [128, 1], F32)
    wtok = c.scratch("g_wtok", [128, 8], F32)
    wfm = c.scratch("g_wfm", [8, T], F32)
    wrow = c.scratch("g_wrow", [1, T], F32)
    for q in range(16):
        qs = slice(128 * q, 128 * (q + 1))
        for i in range(NCH):
            mm(c, scP[:, 0:8], m.XL[i][:, qs], gsb[:, 8 * i:8 * (i + 1)],
               [K(m.XL[i]), K(gsb)], K(scP), i == 0, i == NCH - 1)
        cp(c, sc[:], scP[:, 0:8], [K(scP)], K(sc))
        red(c, st1[:], sc[:], ALU.max, [K(sc)], K(st1))
        ts(c, msk1[:], sc[:], st1[:], ALU.is_equal, [K(sc), K(st1)], K(msk1))
        stt(c, sc[:], msk1[:], f32(-1e30), sc[:], ALU.mult, ALU.add,
            [K(sc), K(msk1)], K(sc))
        red(c, st2[:], sc[:], ALU.max, [K(sc)], K(st2))
        tt(c, p1[:], st1[:], st2[:], ALU.subtract, [K(st1), K(st2)], K(p1))
        act(c, p1[:], p1[:], AF.Sigmoid, [K(p1)], K(p1))
        ts(c, p1[:], p1[:], f32(0.5), ALU.mult, [K(p1)], K(p1))
        ts(c, sc[:], sc[:], st2[:], ALU.is_equal, [K(sc), K(st2)], K(sc))
        ts(c, msk1[:], msk1[:], p1[:], ALU.mult, [K(msk1), K(p1)], K(msk1))
        ts(c, st2[:], p1[:], f32(-1.0), ALU.mult, [K(p1)], K(st2))
        ts(c, st2[:], st2[:], f32(0.5), ALU.add, [K(st2)], K(st2))
        ts(c, sc[:], sc[:], st2[:], ALU.mult, [K(sc), K(st2)], K(sc))
        tt(c, wtok[:], msk1[:], sc[:], ALU.add, [K(msk1), K(sc)], K(wtok))
        trn(c, tpP[0:8, 0:128], wtok[:], m.ident_f[:], K(m.ident_f), [K(wtok)], K(tpP))
        cp(c, wfm[:, qs], tpP[0:8, 0:128], [K(tpP)], K(wfm))

    w12d = dr["l0_w12T"]
    w3d = dr["l0_w3T"]
    wst = [c.scratch(f"l0_wst{i}", [128, 1024], F32) for i in range(2)]
    wsb = [c.scratch(f"l0_wsb{i}", [128, 1024], BF16) for i in range(2)]
    w3st = [c.scratch(f"l0_w3st{i}", [128, 512], F32) for i in range(2)]
    w3res = [c.scratch(f"l0_w3res{i}", [128, 1024], BF16) for i in range(16)]
    h12P = [[m.PS[2 + 2 * a + b] for b in range(2)] for a in range(2)]
    hidT = c.scratch("sh_s2", [128, 512], F32)
    hid = [c.scratch(f"l0_hid{i}", [128, 1024], BF16) for i in range(16)]
    outP = [m.PS[0], m.PS[1]]
    wB = m.PS[6]
    wBs = c.scratch("l0_wBs", [128, 1024], F32)
    tmp_o = c.scratch("sh_s1", [128, 512], F32)

    for e in range(E):
        dma(c, "pool", wrow[:], wfm[e:e + 1, :], K(wrow), rkeys=(K(wfm),))
        for hc in range(16):
            for hf in range(2):
                st = w3st[hf]
                dma(c, "pool", st[:], w3d[e, hc, :, 512 * hf:512 * (hf + 1)],
                    K(st))
                cp(c, w3res[hc][:, 512 * hf:512 * (hf + 1)], st[:], [K(st)],
                   K(w3res[hc]), eng="pool")
        for th in range(2):
            for tb in range(2):
                mm(c, wB[:], m.ones_f[0:1, :],
                   wrow[:, 1024 * th + 512 * tb:1024 * th + 512 * (tb + 1)],
                   [K(m.ones_f), K(wrow)], K(wB), True, True)
                cp(c, wBs[:, 512 * tb:512 * (tb + 1)], wB[:], [K(wB)], K(wBs))
            for hp in range(16):
                for half in range(2):
                    hb = hp + 16 * half
                    buf = wst[(hp * 2 + half) % 2]
                    bufb = wsb[(hp * 2 + half) % 2]
                    dma(c, "sp", buf[:], w12d[e, hb], K(buf))
                    cp(c, bufb[:], buf[:], [K(buf)], K(bufb))
                    for t2 in range(2):
                        tsl = slice(1024 * th + 512 * t2, 1024 * th + 512 * (t2 + 1))
                        for dc in range(NCH):
                            mm(c, h12P[half][t2][:],
                               bufb[:, 128 * dc:128 * (dc + 1)],
                               m.XL[dc][:, tsl],
                               [K(bufb), K(m.XL[dc])], K(h12P[half][t2]),
                               dc == 0, dc == NCH - 1)
                for t2 in range(2):
                    act(c, hidT[:], h12P[0][t2][:], AF.Silu,
                        [K(h12P[0][t2])], K(hidT))
                    tt(c, hid[hp][:, 512 * t2:512 * (t2 + 1)], hidT[:],
                       h12P[1][t2][:], ALU.mult,
                       [K(hidT), K(h12P[1][t2])], K(hid[hp]))
            for dc in range(NCH):
                for t2 in range(2):
                    tsl = slice(1024 * th + 512 * t2, 1024 * th + 512 * (t2 + 1))
                    for hc in range(16):
                        mm(c, outP[t2][:],
                           w3res[hc][:, 128 * dc:128 * (dc + 1)],
                           hid[hc][:, 512 * t2:512 * (t2 + 1)],
                           [K(w3res[hc]), K(hid[hc])], K(outP[t2]),
                           hc == 0, hc == 15)
                    tt(c, tmp_o[:], outP[t2][:],
                       wBs[:, 512 * t2:512 * (t2 + 1)], ALU.mult,
                       [K(outP[t2]), K(wBs)], K(tmp_o))
                    tt(c, m.R[dc][:, tsl], m.R[dc][:, tsl], tmp_o[:], ALU.add,
                       [K(m.R[dc]), K(tmp_o)], K(m.R[dc]))



def emit_l0_attn(c, m):
    H, DM = 8, 64
    dr = m.dr
    PS = m.PS
    PSB = m.PSB
    stair = c.scratch("stair", [128, 1536], BF16)
    dma(c, "sp", stair[:], dr["c_stair"], K(stair))
    trilm = c.scratch("trilm", [128, 128], BF16)
    dma(c, "sp", trilm[:], dr["c_trilm"], K(trilm))
    selT = c.scratch("selT", [72, 1024], F32)
    dma(c, "sp", selT[0:8, :], dr["c_sel"], K(selT))
    dma(c, "sp", selT[64:72, :], dr["c_sel"], K(selT))

    BLC = c.scratch("a_BLC", [72, T], F32)
    bsl, lsl, csl = slice(0, 8), slice(32, 40), slice(64, 72)

    wpt = [c.scratch("a_wpt0", [128, 512], F32) for i in range(2)]
    wpb = [c.scratch(f"a_wpb{i}", [128, 512], BF16) for i in range(2)]
    KN = [c.scratch(f"a_KN{i}", [128, T], BF16) for i in range(4)]
    QN = [c.scratch(f"a_QN{i}", [128, T], BF16) for i in range(4)]
    Vtok = [c.scratch(f"a_vt{h}", [128, 16 * 64], BF16) for h in range(H)]
    vstage = c.scratch("a_vstage", [128, 512], BF16)
    sqn = c.scratch("sh_s0", [128, 512], F32)
    inv = c.scratch("sh_s1", [128, 512], F32)
    ssP = PS[6]

    for kind, wname in (("k", "l0_WkT"), ("q", "l0_WqT"), ("v", "l0_WvT")):
        wd = dr[wname]
        for oc in range(4):
            for dc in range(NCH):
                b = dc % 2
                dma(c, "sp", wpt[b][:], wd[dc], K(wpt[b]))
                cp(c, wpb[b][:], wpt[b][:], [K(wpt[b])], K(wpb[b]))
                for tb in range(4):
                    tsl = slice(512 * tb, 512 * (tb + 1))
                    mm(c, PS[tb][:], wpb[b][:, 128 * oc:128 * (oc + 1)],
                       m.XL[dc][:, tsl], [K(wpb[b]), K(m.XL[dc])], K(PS[tb]),
                       dc == 0, dc == NCH - 1)
            for tb in range(4):
                tsl = slice(512 * tb, 512 * (tb + 1))
                if kind == "v":
                    cp(c, vstage[:], PS[tb][:], [K(PS[tb])], K(vstage))
                    for hh in range(2):
                        h = 2 * oc + hh
                        for jj in range(4):
                            j = 4 * tb + jj
                            trn(c, PSB[:, 0:64],
                                vstage[64 * hh:64 * (hh + 1),
                                       128 * jj:128 * (jj + 1)],
                                m.ident_b[64 * hh:64 * (hh + 1),
                                          64 * hh:64 * (hh + 1)],
                                K(m.ident_b), [K(vstage)], K(PSB))
                            cp(c, Vtok[h][:, 64 * j:64 * (j + 1)],
                               PSB[:, 0:64], [K(PSB)], K(Vtok[h]))
                    continue
                dst = KN if kind == "k" else QN
                act(c, sqn[:], PS[tb][:], AF.Square, [K(PS[tb])], K(sqn))
                for hh in range(2):
                    sl2 = slice(64 * hh, 64 * (hh + 1))
                    mm(c, ssP[sl2, :], m.ones_f[sl2, sl2], sqn[sl2, :],
                       [K(m.ones_f), K(sqn)], K(ssP), True, True)
                act(c, inv[:], ssP[:], AF.Sqrt, [K(ssP)], K(inv))
                ts(c, inv[:], inv[:], f32(1e-12), ALU.max, [K(inv)], K(inv))
                recip(c, inv[:], inv[:], [K(inv)], K(inv))
                tt(c, dst[oc][:, tsl], PS[tb][:], inv[:], ALU.mult,
                   [K(PS[tb]), K(inv)], K(dst[oc]))

    wbt = c.scratch("a_wbt", [128, 8 * NCH], F32)
    wbb = c.scratch("a_wbb", [128, 8 * NCH], BF16)
    dma(c, "sp", wbt[:], dr["l0_WbT"], K(wbt))
    cp(c, wbb[:], wbt[:], [K(wbt)], K(wbb), eng="pool")
    for tb in range(4):
        tsl = slice(512 * tb, 512 * (tb + 1))
        for dc in range(NCH):
            mm(c, PS[6][0:8, :], wbb[:, 8 * dc:8 * (dc + 1)],
               m.XL[dc][:, tsl], [K(wbb), K(m.XL[dc])], K(PS[6]),
               dc == 0, dc == NCH - 1)
        act(c, BLC[bsl, tsl], PS[6][0:8, :], AF.Sigmoid, [K(PS[6])], K(BLC))
        act(c, BLC[lsl, tsl], BLC[bsl, tsl], AF.Ln, [K(BLC)], K(BLC),
            scale=f32(-1.0), bias=f32(1.0 + 1e-8))
    Ltb = [c.scratch(f"a_Ltb{j}", [128, 8], BF16) for j in range(16)]
    Ltf = c.scratch("a_Ltf", [128, 8], F32)
    for j in range(16):
        trn(c, PS[6][:, 0:8], BLC[lsl, 128 * j:128 * (j + 1)],
            m.ident_f[32:40, 32:40], K(m.ident_f), [K(BLC)], K(PS[6]))
        cp(c, Ltf[:], PS[6][:, 0:8], [K(PS[6])], K(Ltf))
        cp(c, Ltb[j][:], Ltf[:], [K(Ltf)], K(Ltb[j]))
    for tb in range(4):
        nj = min(16, 4 * tb + 4)
        for j in range(nj):
            off = min(512 * tb - 128 * j + 384, 1024)
            mm(c, PS[6][0:8, :], Ltb[j][:], stair[:, off:off + 512],
               [K(Ltb[j]), K(stair)], K(PS[6]), j == 0, j == nj - 1)
        cp(c, BLC[csl, 512 * tb:512 * (tb + 1)], PS[6][0:8, :], [K(PS[6])],
           K(BLC))
    cumcol = [c.scratch(f"a_cc{j}", [128, 8], F32) for j in range(16)]
    for j in range(16):
        trn(c, PS[6][:, 0:8], BLC[csl, 128 * j:128 * (j + 1)],
            m.ident_f[64:72, 64:72], K(m.ident_f), [K(BLC)], K(PS[6]))
        ts(c, cumcol[j][:], PS[6][:, 0:8], f32(-1.0), ALU.mult, [K(PS[6])],
           K(cumcol[j]))

    ctxall = [c.scratch(f"a_ctx{i}", [128, T], BF16) for i in range(4)]
    cumBs = c.scratch("a_cumBs", [128, T], F32)
    esb = c.scratch("sh_s0", [128, 512], F32)
    Pt = c.scratch("a_Pt", [128, T], BF16)
    dgb = c.scratch("a_dgb", [128, 128], BF16)
    SP = [PS[4], PS[5]]

    for h in range(H):
        kc, hb = h // 2, 64 * (h % 2)
        hsl = slice(hb, hb + 64)
        for tb in range(4):
            mm(c, PS[6][:], selT[64:72, 128 * h:128 * (h + 1)],
               BLC[csl, 512 * tb:512 * (tb + 1)], [K(selT), K(BLC)],
               K(PS[6]), True, True)
            cp(c, cumBs[:, 512 * tb:512 * (tb + 1)], PS[6][:], [K(PS[6])],
               K(cumBs))
        for j in range(16):
            tw = 128 * (j + 1)
            ncc = (tw + 511) // 512
            for cc in range(ncc):
                w = min(tw - 512 * cc, 512)
                sp = SP[cc % 2]
                mm(c, sp[:, 0:w], KN[kc][hsl, 128 * j:128 * (j + 1)],
                   QN[kc][hsl, 512 * cc:512 * cc + w],
                   [K(KN[kc]), K(QN[kc])], K(sp), True, True)
                c.p.emit("dve", "tensor_scalar", esb[:, 0:w],
                         cumBs[:, 512 * cc:512 * cc + w],
                         cumcol[j][:, h:h + 1], 0.0, ALU.add, ALU.min,
                         reads=(K(cumBs), K(cumcol[j])), writes=(K(esb),))
                act(c, esb[:, 0:w], esb[:, 0:w], AF.Exp,
                    [K(esb)], K(esb), bias=f32(-2.0794415416798357))
                tt(c, Pt[:, 512 * cc:512 * cc + w], esb[:, 0:w], sp[:, 0:w],
                   ALU.mult, [K(esb), K(sp)], K(Pt))
            db = 128 * j
            tt(c, Pt[:, db:db + 128], Pt[:, db:db + 128], trilm[:], ALU.mult,
               [K(Pt), K(trilm)], K(Pt), eng="pool")
            mm(c, PS[6][:, 0:128], selT[0:8, 128 * h:128 * (h + 1)],
               BLC[bsl, db:db + 128], [K(selT), K(BLC)], K(PS[6]), True, True)
            tt(c, dgb[:], m.ident_b[:], PS[6][:, 0:128], ALU.mult,
               [K(m.ident_b), K(PS[6])], K(dgb))
            tt(c, Pt[:, db:db + 128], Pt[:, db:db + 128], dgb[:], ALU.add,
               [K(Pt), K(dgb)], K(Pt))
            if tw % 512:
                c.p.emit("pool", "memset", Pt[:, tw:512 * ncc], 0.0,
                         reads=(), writes=(K(Pt),))
            for cc in range(ncc):
                mm(c, PS[cc][0:64, :], Vtok[h][:, 64 * j:64 * (j + 1)],
                   Pt[:, 512 * cc:512 * (cc + 1)], [K(Vtok[h]), K(Pt)],
                   K(PS[cc]), j == 4 * cc, j == 15)
        for tb in range(4):
            tsl = slice(512 * tb, 512 * (tb + 1))
            cp(c, ctxall[kc][hsl, tsl], PS[tb][0:64, :], [K(PS[tb])],
               K(ctxall[kc]))

    outP2 = [PS[4], PS[5]]
    for dout in range(NCH):
        b = dout % 2
        dma(c, "sp", wpt[b][:], dr["l0_WoT2"][dout], K(wpt[b]))
        cp(c, wpb[b][:], wpt[b][:], [K(wpt[b])], K(wpb[b]))
        for tb in range(4):
            tsl = slice(512 * tb, 512 * (tb + 1))
            op = outP2[tb % 2]
            for ec in range(4):
                mm(c, op[:], wpb[b][:, 128 * ec:128 * (ec + 1)],
                   ctxall[ec][:, tsl], [K(wpb[b]), K(ctxall[ec])], K(op),
                   ec == 0, ec == 3)
            tt(c, m.R[dout][:, tsl], m.R[dout][:, tsl], op[:], ALU.add,
               [K(m.R[dout]), K(op)], K(m.R[dout]))



def emit_l1_attn(c, m):
    dr = m.dr
    PS = m.PS
    PSB = m.PSB
    onesb = c.scratch("ones_b", [128, 128], BF16)
    dma(c, "sp", onesb[:], dr["c_ones_b"], K(onesb))

    wst = c.scratch("p_wst0", [128, 1024], F32)
    wres = [c.scratch(f"p_wres{i}", [128, 1024], BF16) for i in range(NCH)]
    bias_c = c.scratch("p_bias", [128, 8], F32)
    Q = [c.scratch(f"p_Q{i}", [128, T], BF16) for i in range(2)]
    KK = [c.scratch(f"p_K{i}", [128, T], BF16) for i in range(2)]
    Vtok = [c.scratch(f"p_vt{h}", [128, 16 * 65], BF16) for h in range(4)]
    ctxall = [c.scratch(f"p_ctx{i}", [128, T], BF16) for i in range(NCH)]
    vstage = c.scratch("a_vstage", [128, 512], BF16)

    def load_w(wname):
        for dc in range(NCH):
            dma(c, "sp", wst[:], dr[wname][dc], K(wst))
            cp(c, wres[dc][:], wst[:], [K(wst)], K(wres[dc]), eng="pool")

    def proj(wname, bname, scale, kind, hg):
        load_w(wname)
        dma(c, "sp", bias_c[:], dr[bname], K(bias_c))
        if scale != 1.0:
            ts(c, bias_c[:], bias_c[:], f32(scale), ALU.mult, [K(bias_c)],
               K(bias_c))
        for ocl in range(2):
            oc = 2 * hg + ocl
            for tb in range(4):
                tsl = slice(512 * tb, 512 * (tb + 1))
                for dc in range(NCH):
                    mm(c, PS[tb][:], wres[dc][:, 128 * oc:128 * (oc + 1)],
                       m.XL[dc][:, tsl], [K(wres[dc]), K(m.XL[dc])],
                       K(PS[tb]), dc == 0, dc == NCH - 1)
            for tb in range(4):
                tsl = slice(512 * tb, 512 * (tb + 1))
                if kind == "v":
                    act(c, vstage[:], PS[tb][:], AF.Identity,
                        [K(PS[tb]), K(bias_c)], K(vstage),
                        bias=bias_c[:, oc:oc + 1])
                    for hh in range(2):
                        hl = 2 * ocl + hh
                        for jj in range(4):
                            j = 4 * tb + jj
                            trn(c, PSB[:, 0:64],
                                vstage[64 * hh:64 * (hh + 1),
                                       128 * jj:128 * (jj + 1)],
                                m.ident_b[64 * hh:64 * (hh + 1),
                                          64 * hh:64 * (hh + 1)],
                                K(m.ident_b), [K(vstage)], K(PSB))
                            cp(c, Vtok[hl][:, 65 * j:65 * j + 64],
                               PSB[:, 0:64], [K(PSB)], K(Vtok[hl]))
                            c.p.emit("pool", "memset",
                                     Vtok[hl][:, 65 * j + 64:65 * j + 65],
                                     1.0, reads=(), writes=(K(Vtok[hl]),))
                else:
                    dst = Q if kind == "q" else KK
                    act(c, dst[ocl][:, tsl], PS[tb][:], AF.Identity,
                        [K(PS[tb]), K(bias_c)], K(dst[ocl]),
                        bias=bias_c[:, oc:oc + 1],
                        scale=f32(scale) if scale != 1.0 else f32(1.0))

    for hg in range(4):
        proj("l1_wq", "l1_bq", 0.125, "q", hg)
        proj("l1_wk", "l1_bk", 1.0, "k", hg)
        proj("l1_wv", "l1_bv", 1.0, "v", hg)
        emit_softmax_core(c, m, Q, KK, Vtok,
                          ctx=[ctxall[2 * hg + i] for i in range(2)],
                          causal=False)

    load_w("l1_wo")
    dma(c, "sp", bias_c[:], dr["l1_bo"], K(bias_c))
    for dout in range(NCH):
        for tb in range(4):
            tsl = slice(512 * tb, 512 * (tb + 1))
            op = PS[4 + tb % 2]
            for ec in range(NCH):
                mm(c, op[:], wres[ec][:, 128 * dout:128 * (dout + 1)],
                   ctxall[ec][:, tsl], [K(wres[ec]), K(ctxall[ec])], K(op),
                   ec == 0, ec == NCH - 1)
            tt(c, m.R[dout][:, tsl], m.R[dout][:, tsl], op[:], ALU.add,
               [K(m.R[dout]), K(op)], K(m.R[dout]))
        ts(c, m.R[dout][:], m.R[dout][:], bias_c[:, dout:dout + 1], ALU.add,
           [K(m.R[dout]), K(bias_c)], K(m.R[dout]))


def emit_softmax_core(c, m, Q, KK, Vtok, ctx, causal, alb=None, albc=None, mska=None, mskb=None, neginf=None, head_off=0, single_head=None):
    heads = len(Vtok)
    dr = m.dr
    PS = m.PS
    PSB = m.PSB
    onesb = c.scratch("ones_b", [128, 128], BF16)
    stats = c.scratch("s_stats", [128, 20], F32)
    statsb = c.scratch("s_statsb", [128, 16], BF16)
    stgb = c.scratch("s_stgb", [16, 128], BF16)
    nmx = c.scratch("s_nmx", [72, T], BF16)
    Pt = c.scratch("a_Pt", [128, T], BF16)
    srow = c.scratch("s_srow", [72, 512], F32)
    rbc = c.scratch("s_rbc", [64, 512], F32)
    ctxall = ctx

    for h in ([single_head] if single_head is not None else range(heads)):
        kc, hb = h // 2, 64 * (h % 2)
        hsl = slice(hb, hb + 64)
        for qb in range(16):
            qsl = slice(128 * qb, 128 * (qb + 1))
            ns = 4 if not causal else (qb // 4 + 1)
            for sc4 in range(ns):
                w = 512 if not causal else min(512, 128 * (qb + 1) - 512 * sc4)
                sp = PS[sc4]
                mm(c, sp[:, 0:w], Q[kc][hsl, qsl],
                   KK[kc][hsl, 512 * sc4:512 * sc4 + w],
                   [K(Q[kc]), K(KK[kc])], K(sp), True, alb is None)
                if alb is not None:
                    mm(c, sp[:, 0:w], onesb[hb:hb + 1, 0:128],
                       alb[hb:hb + 1, 512 * sc4:512 * sc4 + w],
                       [K(onesb), K(alb)], K(sp), False, True)
                if causal and sc4 == qb // 4:
                    off = 128 * (qb % 4)
                    tt(c, sp[:, off:off + 128], sp[:, off:off + 128],
                       mska[:], ALU.add, [K(sp), K(mska)], K(sp))
                red(c, stats[:, 16 + sc4:17 + sc4], sp[:, 0:w], ALU.max,
                    [K(sp)], K(stats))
            red(c, stats[:, qb:qb + 1], stats[:, 16:16 + ns], ALU.max,
                [K(stats)], K(stats), negate=True)
        cp(c, statsb[:], stats[:, 0:16], [K(stats)], K(statsb))
        trn(c, PSB[0:16, 0:128], statsb[:], m.ident_b[:], K(m.ident_b),
            [K(statsb)], K(PSB))
        cp(c, stgb[:], PSB[0:16, 0:128], [K(PSB)], K(stgb))
        for i in range(16):
            dma(c, "pool", nmx[0:1, 128 * i:128 * (i + 1)], stgb[i:i + 1, :],
                K(nmx), rkeys=(K(stgb),))
            dma(c, "pool", nmx[64:65, 128 * i:128 * (i + 1)], stgb[i:i + 1, :],
                K(nmx), rkeys=(K(stgb),))
        for sb in range(16):
            ssl = slice(128 * sb, 128 * (sb + 1))
            q0 = 0 if not causal else 128 * sb
            for qc in range(4):
                qsl = slice(512 * qc, 512 * (qc + 1))
                if 512 * (qc + 1) <= q0:
                    continue
                sp = PS[4 + qc % 2]
                mm(c, sp[:], KK[kc][hsl, ssl], Q[kc][hsl, qsl],
                   [K(KK[kc]), K(Q[kc])], K(sp), True, False)
                mm(c, sp[:], onesb[hb:hb + 1, 0:128], nmx[hb:hb + 1, qsl],
                   [K(onesb), K(nmx)], K(sp), False, True)
                if causal:
                    for qblk in range(4 * qc, min(sb, 4 * qc + 4)):
                        off = 128 * (qblk % 4)
                        tt(c, sp[:, off:off + 128], sp[:, off:off + 128],
                           neginf[:], ALU.add, [K(sp), K(neginf)], K(sp))
                    if qc == sb // 4:
                        off = 128 * (sb % 4)
                        tt(c, sp[:, off:off + 128], sp[:, off:off + 128],
                           mskb[:], ALU.add, [K(sp), K(mskb)], K(sp))
                if albc is not None:
                    act(c, Pt[:, qsl], sp[:], AF.Exp, [K(sp), K(albc)], K(Pt),
                        bias=albc[:, 16 * (h + head_off) + sb:
                                  16 * (h + head_off) + sb + 1])
                else:
                    act(c, Pt[:, qsl], sp[:], AF.Exp, [K(sp)], K(Pt))
            for qb in range(4):
                qsl = slice(512 * qb, 512 * (qb + 1))
                if causal and 512 * (qb + 1) <= q0:
                    continue
                last_sb = (4 * qb + 3) if causal else 15
                mm(c, PS[qb][0:65, :], Vtok[h][:, 65 * sb:65 * sb + 65],
                   Pt[:, qsl], [K(Vtok[h]), K(Pt)], K(PS[qb]),
                   sb == 0, sb == last_sb)
        for qb in range(4):
            qsl = slice(512 * qb, 512 * (qb + 1))
            cp(c, srow[64:65, :], PS[qb][64:65, :], [K(PS[qb])], K(srow))
            recip(c, srow[64:65, :], srow[64:65, :], [K(srow)], K(srow))
            mm(c, PS[5][0:64, :], m.ones_f[64:65, 0:64], srow[64:65, :],
               [K(m.ones_f), K(srow)], K(PS[5]), True, True)
            cp(c, rbc[:], PS[5][0:64, :], [K(PS[5])], K(rbc))
            tt(c, ctxall[kc][hsl, qsl], PS[qb][0:64, :], rbc[:], ALU.mult,
               [K(PS[qb]), K(rbc)], K(ctxall[kc]))



def emit_l1_ffn(c, m):
    E = 8
    dr = m.dr
    PS = m.PS
    idxf = c.scratch("f_idxf", [128, 8], F32)
    ewt = c.scratch("f_ewt", [128, 8], F32)
    wcols = c.scratch("f_wcols", [128, 4 * E], F32)
    eqm = c.scratch("f_eqm", [128, 2], F32)
    dma(c, "sp", idxf[:], dr["l1_idxf"], K(idxf))
    dma(c, "sp", ewt[:], dr["l1_ewf"], K(ewt))
    for e in range(E):
        for t4 in range(4):
            isl = slice(2 * t4, 2 * t4 + 2)
            ts(c, eqm[:], idxf[:, isl], float(e), ALU.is_equal,
               [K(idxf)], K(eqm))
            tt(c, eqm[:], eqm[:], ewt[:, isl], ALU.mult,
               [K(eqm), K(ewt)], K(eqm))
            red(c, wcols[:, 4 * e + t4:4 * e + t4 + 1], eqm[:], ALU.add,
                [K(eqm)], K(wcols))
    ts(c, wcols[:], wcols[:], f32(0.5), ALU.mult, [K(wcols)], K(wcols))
    wq4 = c.scratch("f_wq4", [4, 128], F32)
    wrow = c.scratch("g_wrow", [1, T], F32)
    wBs = c.scratch("l0_wBs", [128, 1024], F32)
    wfm = c.scratch("g_wfm", [8, T], F32)
    for e in range(E):
        trn(c, PS[3][0:4, 0:128], wcols[:, 4 * e:4 * e + 4], m.ident_f[:],
            K(m.ident_f), [K(wcols)], K(PS[3]))
        cp(c, wq4[:], PS[3][0:4, 0:128], [K(PS[3])], K(wq4))
        for i in range(4):
            dma(c, "pool", wfm[e:e + 1, 128 * i:128 * (i + 1)],
                wq4[i:i + 1, :], K(wfm), rkeys=(K(wq4),))

    wst = [c.scratch(f"p_wst{i}", [128, 1024], F32) for i in range(2)]
    wsb = [c.scratch(f"f_wsb{i}", [128, 1024], BF16) for i in range(2)]
    hid = [c.scratch(f"f_hid{i}", [128, 512], BF16) for i in range(32)]
    tmp_o = c.scratch("sh_s2", [128, 512], F32)
    for e in range(E):
        dma(c, "pool", wrow[0:1, 0:512], wfm[e:e + 1, 0:512], K(wrow),
            rkeys=(K(wfm),))
        mm(c, PS[6][:], m.ones_f[0:1, :], wrow[0:1, 0:512],
           [K(m.ones_f), K(wrow)], K(PS[6]), True, True)
        cp(c, wBs[:, 0:512], PS[6][:], [K(PS[6])], K(wBs))
        for ffb in range(32):
            b = ffb % 2
            dma(c, "sp", wst[b][:], dr["l1_w1T"][e, ffb], K(wst[b]))
            cp(c, wsb[b][:], wst[b][:], [K(wst[b])], K(wsb[b]))
            ph = PS[ffb % 2]
            for dc in range(NCH):
                mm(c, ph[:], wsb[b][:, 128 * dc:128 * (dc + 1)],
                   m.XL[dc][:, 0:512], [K(wsb[b]), K(m.XL[dc])], K(ph),
                   dc == 0, dc == NCH - 1)
            act(c, hid[ffb][:], ph[:], AF.Gelu, [K(ph)], K(hid[ffb]))
        for dh in range(2):
            for ffc in range(32):
                b = ffc % 2
                dma(c, "sp", wst[b][:], dr["l1_w2"][e, ffc], K(wst[b]))
                cp(c, wsb[b][:], wst[b][:], [K(wst[b])], K(wsb[b]))
                for dd in range(4):
                    dout = 4 * dh + dd
                    mm(c, PS[2 + dd][:],
                       wsb[b][:, 128 * dout:128 * (dout + 1)], hid[ffc][:],
                       [K(wsb[b]), K(hid[ffc])], K(PS[2 + dd]),
                       ffc == 0, ffc == 31)
            for dd in range(4):
                dout = 4 * dh + dd
                tt(c, tmp_o[:], PS[2 + dd][:], wBs[:, 0:512], ALU.mult,
                   [K(PS[2 + dd]), K(wBs)], K(tmp_o))
                tt(c, m.R[dout][:, 0:512], m.R[dout][:, 0:512], tmp_o[:],
                   ALU.add, [K(m.R[dout]), K(tmp_o)], K(m.R[dout]))
    bcol = c.scratch("f_bcol", [128, 8], F32)
    dma(c, "sp", bcol[:], dr["l1_bias"], K(bcol))
    ts(c, bcol[:], bcol[:], f32(0.5), ALU.mult, [K(bcol)], K(bcol))
    for dc in range(NCH):
        ts(c, m.R[dc][:], m.R[dc][:], bcol[:, dc:dc + 1], ALU.add,
           [K(m.R[dc]), K(bcol)], K(m.R[dc]))



def emit_l2_ffn(c, m):
    import os
    _stage = int(os.environ.get("L2F_STAGE", "9"))
    dr = m.dr
    PS = m.PS
    PSB = m.PSB
    E = 8
    wst = c.scratch("p_wst0", [128, 1024], F32)
    wres8 = [c.scratch(f"p_wres{i}", [128, 1024], BF16) for i in range(NCH)]
    Wup = c.scratch("e_wup", [64, 2048], BF16)
    Wdn = c.scratch("e_wdn", [128, 512], BF16)
    Wg = c.scratch("e_wg", [128, 64], BF16)

    def load_wres(name):
        for dc in range(NCH):
            dma(c, "sp", wst[:], dr[name][dc], K(wst))
            cp(c, wres8[dc][:], wst[:], [K(wst)], K(wres8[dc]), eng="pool")

    dma(c, "sp", wst[:, 0:512], dr["l2_Wdn"], K(wst))
    cp(c, Wdn[:], wst[:, 0:512], [K(wst)], K(Wdn), eng="pool")
    dma(c, "sp", wst[:, 0:64], dr["l2_Wg"], K(wst))
    cp(c, Wg[:], wst[:, 0:64], [K(wst)], K(Wg), eng="pool")
    dma(c, "sp", wst[0:64, :], dr["l2_Wup"][:, 0:1024], K(wst))
    cp(c, Wup[:, 0:1024], wst[0:64, :], [K(wst)], K(Wup), eng="pool")
    dma(c, "sp", wst[0:64, :], dr["l2_Wup"][:, 1024:2048], K(wst))
    cp(c, Wup[:, 1024:2048], wst[0:64, :], [K(wst)], K(Wup), eng="pool")

    cols = c.scratch("e_cols", [128, 33], F32)
    dma(c, "sp", cols[:, 0:8], dr["l2_bq"], K(cols))
    dma(c, "sp", cols[:, 8:16], dr["l2_bo"], K(cols))
    dma(c, "sp", cols[:, 16:24], dr["l2_g1"], K(cols))
    dma(c, "sp", cols[:, 24:32], dr["l2_g2"], K(cols))
    dma(c, "sp", cols[0:64, 32:33], dr["l2_bdn"], K(cols))
    ts(c, cols[:, 0:8], cols[:, 0:8], f32(0.125), ALU.mult, [K(cols)], K(cols))
    bup_c = c.scratch("e_bupc", [128, 16], F32)
    dma(c, "sp", bup_c[:], dr["l2_bup"], K(bup_c))
    bgB = c.scratch("e_bgB", [128, 8], F32)
    dma(c, "sp", bgB[:], dr["l2_bgB"], K(bgB))

    hbuf = [c.scratch(f"e_h{i}", [128, 512], BF16) for i in range(NCH)]
    h2full = [c.scratch(f"e_h2f{i}", [128, 1024], BF16) for i in range(NCH)]
    h2e = [c.scratch(f"e_h2e{i}", [128, 512], BF16) for i in range(NCH)]
    qf = c.scratch("e_qf", [128, 512], BF16)
    kvf = c.scratch("e_kvf", [128, 512], BF16)
    qtok = [c.scratch(f"e_qt{j}", [128, 1024], BF16) for j in range(4)]
    x2 = [qtok[i // 2][:, 512 * (i % 2):512 * (i % 2) + 512] for i in range(NCH)]
    x2k = [K(qtok[i // 2]) for i in range(NCH)]
    ktok = [c.scratch(f"e_kt{j}", [128, 1024], BF16) for j in range(4)]
    vtok = [c.scratch(f"e_vt2{j}", [128, 1024], BF16) for j in range(4)]
    prod = c.scratch("e_prod", [128, 1040], BF16)
    prod2 = c.scratch("e_prod2", [128, 1024], BF16)
    satt = c.scratch("e_satt", [128, 16], F32)
    sne = c.scratch("e_sne", [128, 2], F32)
    attn_t = c.scratch("e_attn", [128, 16], BF16)
    aout = c.scratch("e_aout", [128, 1024], BF16)
    aoutf = c.scratch("e_aoutf", [128, 64], F32)
    afm = [ktok[i // 2][:, 512 * (i % 2):512 * (i % 2) + 512]
           for i in range(NCH)]
    afmk = [K(ktok[i // 2]) for i in range(NCH)]
    sq = c.scratch("sh_s0", [128, 512], F32)
    rstd = c.scratch("sh_s1", [128, 512], F32)
    gtok = c.scratch("e_gtok", [128, 8], F32)
    wfm = c.scratch("g_wfm", [8, T], F32)
    wrow = c.scratch("e_wrow", [1, 512], F32)
    wBs = c.scratch("e_wBs", [128, 512], F32)
    for g in range(16):
        c.p.emit("pool", "memset", prod[:, 65 * g + 64:65 * g + 65], 0.0,
                 reads=(), writes=(K(prod),))

    def rmsnorm(src_aps, src_keys, goff, dst_aps, dst_keys):
        for dc in range(NCH):
            act(c, sq[:], src_aps[dc], AF.Square, [src_keys[dc]], K(sq))
            mm(c, PS[6][:], m.ones_f[:], sq[:], [K(m.ones_f), K(sq)],
               K(PS[6]), dc == 0, dc == NCH - 1)
        act(c, rstd[:], PS[6][:], AF.Sqrt, [K(PS[6])], K(rstd),
            scale=f32(1.0 / D), bias=f32(1e-6))
        recip(c, rstd[:], rstd[:], [K(rstd)], K(rstd))
        for dc in range(NCH):
            tt(c, sq[:], src_aps[dc], rstd[:], ALU.mult,
               [src_keys[dc], K(rstd)], K(sq))
            ts(c, dst_aps[dc], sq[:], cols[:, goff + dc:goff + dc + 1],
               ALU.mult, [K(sq), K(cols)], dst_keys[dc])

    for tb in range(4):
        tsl = slice(512 * tb, 512 * (tb + 1))
        rmsnorm([m.XL[dc][:, tsl] for dc in range(NCH)],
                [K(m.XL[dc]) for dc in range(NCH)], 16,
                [h[:] for h in hbuf], [K(h) for h in hbuf])
        load_wres("l2_Wq")
        for oc in range(NCH):
            for dc in range(NCH):
                mm(c, PS[0][:], wres8[dc][:, 128 * oc:128 * (oc + 1)],
                   hbuf[dc][:], [K(wres8[dc]), K(hbuf[dc])], K(PS[0]),
                   dc == 0, dc == NCH - 1)
            act(c, qf[:], PS[0][:], AF.Identity, [K(PS[0]), K(cols)], K(qf),
                bias=cols[:, oc:oc + 1], scale=f32(0.125))
            for jj in range(4):
                trn(c, PSB[:, 0:128], qf[:, 128 * jj:128 * (jj + 1)],
                    m.ident_b[:], K(m.ident_b), [K(qf)], K(PSB))
                cp(c, qtok[jj][:, 128 * oc:128 * (oc + 1)], PSB[:, 0:128],
                   [K(PSB)], K(qtok[jj]))
        if _stage < 2:
            continue
        for dc in range(NCH):
            mm(c, PS[0][0:64, :], Wdn[:, 64 * dc:64 * (dc + 1)], hbuf[dc][:],
               [K(Wdn), K(hbuf[dc])], K(PS[0]), dc == 0, dc == NCH - 1)
        zb = c.scratch("e_zb", [64, 512], BF16)
        act(c, zb[:], PS[0][0:64, :], AF.Identity, [K(PS[0]), K(cols)], K(zb),
            bias=cols[0:64, 32:33])
        for ec in range(16):
            mm(c, PS[1][:], Wup[:, 128 * ec:128 * (ec + 1)], zb[:],
               [K(Wup), K(zb)], K(PS[1]), True, True)
            act(c, kvf[:], PS[1][:], AF.Identity, [K(PS[1]), K(bup_c)],
                K(kvf), bias=bup_c[:, ec:ec + 1])
            for jj in range(4):
                trn(c, PSB[:, 0:64], kvf[0:64, 128 * jj:128 * (jj + 1)],
                    m.ident_b[0:64, 0:64], K(m.ident_b), [K(kvf)], K(PSB))
                cp(c, ktok[jj][:, 64 * ec:64 * (ec + 1)], PSB[:, 0:64],
                   [K(PSB)], K(ktok[jj]))
                trn(c, PSB[:, 64:128], kvf[64:128, 128 * jj:128 * (jj + 1)],
                    m.ident_b[64:128, 64:128], K(m.ident_b), [K(kvf)], K(PSB))
                cp(c, vtok[jj][:, 64 * ec:64 * (ec + 1)], PSB[:, 64:128],
                   [K(PSB)], K(vtok[jj]))
        if _stage < 3:
            continue
        for jj in range(4):
            for h in range(16):
                pr_out = bass.AP(prod, 0, [[1040, 128], [65, 16], [1, 64]])
                k_in = bass.AP(ktok[jj], 0, [[1024, 128], [64, 16], [1, 64]])
                q_in = bass.AP(qtok[jj], 64 * h, [[1024, 128], [0, 16], [1, 64]])
                c.p.emit("dve", "tensor_tensor", pr_out, k_in, q_in, ALU.mult,
                         reads=(K(ktok[jj]), K(qtok[jj])), writes=(K(prod),))
                pr_in = bass.AP(prod, 0, [[1040, 128], [65, 16], [1, 64]])
                so = bass.AP(satt, 0, [[16, 128], [1, 16]])
                c.p.emit("dve", "tensor_reduce", so, pr_in, AX.X, ALU.add,
                         reads=(K(prod),), writes=(K(satt),))
                red(c, sne[:, 0:1], satt[:], ALU.max, [K(satt)], K(sne),
                    negate=True)
                act(c, satt[:], satt[:], AF.Exp, [K(satt), K(sne)], K(satt),
                    bias=sne[:, 0:1])
                red(c, sne[:, 1:2], satt[:], ALU.add, [K(satt)], K(sne))
                recip(c, sne[:, 1:2], sne[:, 1:2], [K(sne)], K(sne))
                ts(c, attn_t[:], satt[:], sne[:, 1:2], ALU.mult,
                   [K(satt), K(sne)], K(attn_t))
                at_in = bass.AP(attn_t, 0, [[16, 128], [1, 16], [0, 64]])
                c.p.emit("dve", "tensor_tensor", prod2[:], vtok[jj][:], at_in,
                         ALU.mult, reads=(K(vtok[jj]), K(attn_t)),
                         writes=(K(prod2),))
                p2_in = bass.AP(prod2, 0, [[1024, 128], [1, 64], [64, 16]])
                c.p.emit("dve", "tensor_reduce", aoutf[:], p2_in, AX.X,
                         ALU.add, reads=(K(prod2),), writes=(K(aoutf),))
                cp(c, aout[:, 64 * h:64 * (h + 1)], aoutf[:], [K(aoutf)],
                   K(aout), eng="pool")
            for oc in range(NCH):
                trn(c, PSB[:, 0:128], aout[:, 128 * oc:128 * (oc + 1)],
                    m.ident_b[:], K(m.ident_b), [K(aout)], K(PSB))
                cp(c, afm[oc][:, 128 * jj:128 * (jj + 1)], PSB[:, 0:128],
                   [K(PSB)], afmk[oc])
        if _stage < 4:
            continue
        load_wres("l2_Wo")
        for dout in range(NCH):
            for ec in range(NCH):
                mm(c, PS[0][:], wres8[ec][:, 128 * dout:128 * (dout + 1)],
                   afm[ec], [K(wres8[ec]), afmk[ec]], K(PS[0]),
                   ec == 0, ec == NCH - 1)
            act(c, sq[:], PS[0][:], AF.Identity, [K(PS[0]), K(cols)], K(sq),
                bias=cols[:, 8 + dout:8 + dout + 1])
            tt(c, rstd[:], sq[:], m.XL[dout][:, tsl], ALU.add,
               [K(sq), K(m.XL[dout])], K(rstd))
            cp(c, x2[dout], rstd[:], [K(rstd)], x2k[dout])
            stt(c, m.R[dout][:, tsl], rstd[:], f32(0.5), m.R[dout][:, tsl],
                ALU.mult, ALU.add, [K(rstd), K(m.R[dout])], K(m.R[dout]))
        if _stage < 5:
            continue
        lsl = slice(512 * (tb % 2), 512 * (tb % 2) + 512)
        rmsnorm([x2[dc] for dc in range(NCH)],
                x2k, 24,
                [h2full[dc][:, lsl] for dc in range(NCH)],
                [K(h2full[dc]) for dc in range(NCH)])
        for jj in range(4):
            qsl = slice(512 * tb + 128 * jj, 512 * tb + 128 * (jj + 1))
            lq = slice(512 * (tb % 2) + 128 * jj,
                       512 * (tb % 2) + 128 * (jj + 1))
            for dc in range(NCH):
                mm(c, PS[0][:, 0:8], h2full[dc][:, lq],
                   Wg[:, 8 * dc:8 * (dc + 1)], [K(h2full[dc]), K(Wg)],
                   K(PS[0]), dc == 0, dc == NCH - 1)
            tt(c, gtok[:], PS[0][:, 0:8], bgB[:], ALU.add,
               [K(PS[0]), K(bgB)], K(gtok))
            red(c, sne[:, 0:1], gtok[:], ALU.max, [K(gtok)], K(sne),
                negate=True)
            act(c, gtok[:], gtok[:], AF.Exp, [K(gtok), K(sne)], K(gtok),
                bias=sne[:, 0:1])
            red(c, sne[:, 1:2], gtok[:], ALU.add, [K(gtok)], K(sne))
            recip(c, sne[:, 1:2], sne[:, 1:2], [K(sne)], K(sne))
            ts(c, sne[:, 1:2], sne[:, 1:2], f32(0.5), ALU.mult, [K(sne)],
               K(sne))
            ts(c, gtok[:], gtok[:], sne[:, 1:2], ALU.mult,
               [K(gtok), K(sne)], K(gtok))
            trn(c, PS[4][0:8, 0:128], gtok[:], m.ident_f[:], K(m.ident_f),
                [K(gtok)], K(PS[4]))
            cp(c, wfm[:, qsl], PS[4][0:8, 0:128], [K(PS[4])], K(wfm))

        if tb % 2 == 1 and _stage >= 6:
            for e in range(E):
                for dc in range(NCH):
                    dma(c, "sp", wst[:], dr["l2_We"][e, dc], K(wst))
                    cp(c, wres8[dc][:], wst[:], [K(wst)], K(wres8[dc]),
                       eng="pool")
                for tb2 in (tb - 1, tb):
                    gsl = slice(512 * tb2, 512 * (tb2 + 1))
                    l2s = slice(512 * (tb2 % 2), 512 * (tb2 % 2) + 512)
                    dma(c, "pool", wrow[:], wfm[e:e + 1, gsl], K(wrow),
                        rkeys=(K(wfm),))
                    mm(c, PS[6][:], m.ones_f[0:1, :], wrow[:],
                       [K(m.ones_f), K(wrow)], K(PS[6]), True, True)
                    cp(c, wBs[:, 0:512], PS[6][:], [K(PS[6])], K(wBs))
                    for dc in range(NCH):
                        tt(c, h2e[dc][:], h2full[dc][:, l2s], wBs[:, 0:512],
                           ALU.mult, [K(h2full[dc]), K(wBs)], K(h2e[dc]))
                    for dout in range(NCH):
                        op = PS[dout % 2]
                        for dc in range(NCH):
                            mm(c, op[:],
                               wres8[dc][:, 128 * dout:128 * (dout + 1)],
                               h2e[dc][:], [K(wres8[dc]), K(h2e[dc])], K(op),
                               dc == 0, dc == NCH - 1)
                        tt(c, m.R[dout][:, gsl], m.R[dout][:, gsl], op[:],
                           ALU.add, [K(m.R[dout]), K(op)], K(m.R[dout]))




def emit_l2_attn(c, m):
    dr = m.dr
    PS = m.PS
    PSB = m.PSB
    onesb = c.scratch("ones_b", [128, 128], BF16)
    dma(c, "sp", onesb[:], dr["c_ones_b"], K(onesb))
    wst = c.scratch("p_wst0", [128, 1024], F32)
    wres8 = [c.scratch(f"p_wres{i}", [128, 1024], BF16) for i in range(NCH)]
    QQ = [c.scratch(f"p_Q{i}", [128, T], BF16) for i in range(2)]
    KK = [c.scratch(f"p_K{i}", [128, T], BF16) for i in range(2)]
    Vtok = [c.scratch(f"p_vt{h}", [128, 16 * 65], BF16) for h in range(2)]
    ctmp = [c.scratch(f"l2a_ctmp{i}", [128, T], BF16) for i in range(2)]
    ctxall = [c.scratch(f"p_ctx{i}", [128, T], BF16) for i in range(NCH)]
    vstage = c.scratch("a_vstage", [128, 512], BF16)
    lrow = c.scratch("l2a_lrow", [1, 128], F32)
    nlamB = c.scratch("l2a_nlam", [128, 1], F32)
    snw_c = c.scratch("l2a_snwc", [128, 1], F32)
    of = c.scratch("sh_s0", [128, 512], F32)
    sq2 = c.scratch("sh_s1", [128, 512], F32)
    rr = c.scratch("sh_s2", [128, 512], F32)
    bias_c = c.scratch("p_bias", [128, 8], F32)

    dma(c, "sp", lrow[0:1, 0:32], dr["l2a_lq1"], K(lrow))
    dma(c, "sp", lrow[0:1, 32:64], dr["l2a_lk1"], K(lrow))
    dma(c, "sp", lrow[0:1, 64:96], dr["l2a_lq2"], K(lrow))
    dma(c, "sp", lrow[0:1, 96:128], dr["l2a_lk2"], K(lrow))
    dma(c, "sp", snw_c[:], dr["l2a_snw"], K(snw_c))
    ts(c, snw_c[:], snw_c[:], f32(0.8), ALU.mult, [K(snw_c)], K(snw_c))
    lp = c.scratch("l2a_lp", [1, 68], F32)
    tt(c, lp[0:1, 0:32], lrow[0:1, 0:32], lrow[0:1, 32:64], ALU.mult,
       [K(lrow)], K(lp))
    tt(c, lp[0:1, 32:64], lrow[0:1, 64:96], lrow[0:1, 96:128], ALU.mult,
       [K(lrow)], K(lp))
    red(c, lp[0:1, 64:65], lp[0:1, 0:32], ALU.add, [K(lp)], K(lp))
    red(c, lp[0:1, 65:66], lp[0:1, 32:64], ALU.add, [K(lp)], K(lp))
    act(c, lp[0:1, 64:66], lp[0:1, 64:66], AF.Exp, [K(lp)], K(lp))
    tt(c, lp[0:1, 66:67], lp[0:1, 64:65], lp[0:1, 65:66], ALU.subtract,
       [K(lp)], K(lp))
    ts(c, lp[0:1, 66:67], lp[0:1, 66:67], f32(0.2), ALU.add, [K(lp)], K(lp))
    ts(c, lp[0:1, 66:67], lp[0:1, 66:67], f32(-1.0), ALU.mult, [K(lp)], K(lp))
    mm(c, PS[6][:, 0:1], m.ones_f[0:1, :], lp[0:1, 66:67],
       [K(m.ones_f), K(lp)], K(PS[6]), True, True)
    cp(c, nlamB[:], PS[6][:, 0:1], [K(PS[6])], K(nlamB))

    for g in range(8):
        for dc in range(NCH):
            dma(c, "sp", wst[:, 0:640], dr["l2a_wqkvT"][g, dc], K(wst))
            cp(c, wres8[dc][:, 0:640], wst[:, 0:640], [K(wst)],
               K(wres8[dc]), eng="pool")
        for oc in range(5):
            for tb in range(4):
                tsl = slice(512 * tb, 512 * (tb + 1))
                for dc in range(NCH):
                    mm(c, PS[tb][:], wres8[dc][:, 128 * oc:128 * (oc + 1)],
                       m.XL[dc][:, tsl], [K(wres8[dc]), K(m.XL[dc])],
                       K(PS[tb]), dc == 0, dc == NCH - 1)
            for tb in range(4):
                tsl = slice(512 * tb, 512 * (tb + 1))
                if oc == 4:
                    cp(c, vstage[:], PS[tb][:], [K(PS[tb])], K(vstage))
                    for hh in range(2):
                        for jj in range(4):
                            j = 4 * tb + jj
                            trn(c, PSB[:, 0:64],
                                vstage[64 * hh:64 * (hh + 1),
                                       128 * jj:128 * (jj + 1)],
                                m.ident_b[64 * hh:64 * (hh + 1),
                                          64 * hh:64 * (hh + 1)],
                                K(m.ident_b), [K(vstage)], K(PSB))
                            cp(c, Vtok[hh][:, 65 * j:65 * j + 64],
                               PSB[:, 0:64], [K(PSB)], K(Vtok[hh]))
                            c.p.emit("pool", "memset",
                                     Vtok[hh][:, 65 * j + 64:65 * j + 65],
                                     1.0, reads=(), writes=(K(Vtok[hh]),))
                elif oc < 2:
                    act(c, QQ[oc][:, tsl], PS[tb][:], AF.Identity,
                        [K(PS[tb])], K(QQ[oc]),
                        scale=f32(0.17677669529663687))
                else:
                    cp(c, KK[oc - 2][:, tsl], PS[tb][:], [K(PS[tb])],
                       K(KK[oc - 2]))
        emit_softmax_core(c, m, QQ, KK, [Vtok[0], Vtok[1], Vtok[0], Vtok[1]],
                          ctx=ctmp, causal=False)
        for hh in range(2):
            hsl = slice(64 * hh, 64 * (hh + 1))
            for tb in range(4):
                tsl = slice(512 * tb, 512 * (tb + 1))
                stt(c, of[0:64, :], ctmp[1][hsl, tsl], nlamB[hsl, :],
                    ctmp[0][hsl, tsl], ALU.mult, ALU.add,
                    [K(ctmp[0]), K(ctmp[1]), K(nlamB)], K(of))
                act(c, sq2[0:64, :], of[0:64, :], AF.Square, [K(of)], K(sq2))
                mm(c, PS[6][0:64, :], m.ones_f[0:64, 0:64], sq2[0:64, :],
                   [K(m.ones_f), K(sq2)], K(PS[6]), True, True)
                act(c, rr[0:64, :], PS[6][0:64, :], AF.Sqrt, [K(PS[6])],
                    K(rr), scale=f32(1.0 / 64.0), bias=f32(1e-5))
                recip(c, rr[0:64, :], rr[0:64, :], [K(rr)], K(rr))
                ts(c, rr[0:64, :], rr[0:64, :], snw_c[0:64, :], ALU.mult,
                   [K(rr), K(snw_c)], K(rr))
                tt(c, ctxall[g][hsl, tsl], of[0:64, :], rr[0:64, :],
                   ALU.mult, [K(of), K(rr)], K(ctxall[g]))

    for dc in range(NCH):
        dma(c, "sp", wst[:], dr["l2a_wprojT"][dc], K(wst))
        cp(c, wres8[dc][:], wst[:], [K(wst)], K(wres8[dc]), eng="pool")
    dma(c, "sp", bias_c[:], dr["l2a_bproj"], K(bias_c))
    for dout in range(NCH):
        for tb in range(4):
            tsl = slice(512 * tb, 512 * (tb + 1))
            op = PS[4 + tb % 2]
            for ec in range(NCH):
                mm(c, op[:], wres8[ec][:, 128 * dout:128 * (dout + 1)],
                   ctxall[ec][:, tsl], [K(wres8[ec]), K(ctxall[ec])], K(op),
                   ec == 0, ec == NCH - 1)
            tt(c, m.R[dout][:, tsl], m.R[dout][:, tsl], op[:], ALU.add,
               [K(m.R[dout]), K(op)], K(m.R[dout]))
        ts(c, m.R[dout][:], m.R[dout][:], bias_c[:, dout:dout + 1], ALU.add,
           [K(m.R[dout]), K(bias_c)], K(m.R[dout]))



def emit_l3_attn(c, m):
    dr = m.dr
    PS = m.PS
    PSB = m.PSB
    onesb = c.scratch("ones_b", [128, 128], BF16)
    dma(c, "sp", onesb[:], dr["c_ones_b"], K(onesb))
    wst = c.scratch("p_wst0", [128, 1024], F32)
    wres = [c.scratch(f"p_wres{i}", [128, 1024], BF16) for i in range(NCH)]
    Q = [c.scratch(f"p_Q{i}", [128, T], BF16) for i in range(2)]
    KK = [c.scratch(f"p_K{i}", [128, T], BF16) for i in range(2)]
    Vtok = [c.scratch(f"p_vt{h}", [128, 16 * 65], BF16) for h in range(4)]
    ctxall = [c.scratch(f"p_ctx{i}", [128, T], BF16) for i in range(NCH)]
    vstage = c.scratch("a_vstage", [128, 512], BF16)
    alb = c.scratch("s_alb", [65, T], BF16)
    albc = c.scratch("s_albc", [128, 256], F32)
    mska = c.scratch("s_mska", [128, 128], F32)
    mskb = c.scratch("s_mskb", [128, 128], F32)
    neginf = c.scratch("s_neginf", [128, 128], F32)
    dma(c, "sp", albc[:], dr["l3_alibic"], K(albc))
    dma(c, "sp", mska[:], dr["l3_mska"], K(mska))
    dma(c, "sp", mskb[:], dr["l3_mskb"], K(mskb))
    dma(c, "sp", neginf[:], dr["l3_neginf"], K(neginf))

    def load_w(wname, part):
        for dc in range(NCH):
            dma(c, "sp", wst[:], dr[wname][part, dc], K(wst))
            cp(c, wres[dc][:], wst[:], [K(wst)], K(wres[dc]), eng="pool")

    for hg in range(4):
        for kind in range(3):
            for dc in range(NCH):
                dma(c, "sp", wst[:, 0:256],
                    dr["l3_wqkv"][dc, :, 1024 * kind + 256 * hg:
                                  1024 * kind + 256 * (hg + 1)], K(wst))
                cp(c, wres[dc][:, 0:256], wst[:, 0:256], [K(wst)],
                   K(wres[dc]), eng="pool")
            for ocl in range(2):
                for tb in range(4):
                    tsl = slice(512 * tb, 512 * (tb + 1))
                    for dc in range(NCH):
                        mm(c, PS[tb][:],
                           wres[dc][:, 128 * ocl:128 * (ocl + 1)],
                           m.XL[dc][:, tsl], [K(wres[dc]), K(m.XL[dc])],
                           K(PS[tb]), dc == 0, dc == NCH - 1)
                for tb in range(4):
                    tsl = slice(512 * tb, 512 * (tb + 1))
                    if kind == 2:
                        cp(c, vstage[:], PS[tb][:], [K(PS[tb])], K(vstage))
                        for hh in range(2):
                            hl = 2 * ocl + hh
                            for jj in range(4):
                                j = 4 * tb + jj
                                trn(c, PSB[:, 0:64],
                                    vstage[64 * hh:64 * (hh + 1),
                                           128 * jj:128 * (jj + 1)],
                                    m.ident_b[64 * hh:64 * (hh + 1),
                                              64 * hh:64 * (hh + 1)],
                                    K(m.ident_b), [K(vstage)], K(PSB))
                                cp(c, Vtok[hl][:, 65 * j:65 * j + 64],
                                   PSB[:, 0:64], [K(PSB)], K(Vtok[hl]))
                                c.p.emit("pool", "memset",
                                         Vtok[hl][:, 65 * j + 64:65 * j + 65],
                                         1.0, reads=(), writes=(K(Vtok[hl]),))
                    elif kind == 0:
                        act(c, Q[ocl][:, tsl], PS[tb][:], AF.Identity,
                            [K(PS[tb])], K(Q[ocl]), scale=f32(0.125))
                    else:
                        cp(c, KK[ocl][:, tsl], PS[tb][:], [K(PS[tb])],
                           K(KK[ocl]))
        emit_softmax_core_l3(c, m, Q, KK, Vtok, ctxall, hg, alb, albc,
                             mska, mskb, neginf)

    for dc in range(NCH):
        dma(c, "sp", wst[:], dr["l3_wproj"][dc], K(wst))
        cp(c, wres[dc][:], wst[:], [K(wst)], K(wres[dc]), eng="pool")
    for dout in range(NCH):
        for tb in range(4):
            tsl = slice(512 * tb, 512 * (tb + 1))
            op = PS[4 + tb % 2]
            for ec in range(NCH):
                mm(c, op[:], wres[ec][:, 128 * dout:128 * (dout + 1)],
                   ctxall[ec][:, tsl], [K(wres[ec]), K(ctxall[ec])], K(op),
                   ec == 0, ec == NCH - 1)
            tt(c, m.R[dout][:, tsl], m.R[dout][:, tsl], op[:], ALU.add,
               [K(m.R[dout]), K(op)], K(m.R[dout]))


def emit_softmax_core_l3(c, m, Q, KK, Vtok, ctxall, hg, alb, albc,
                         mska, mskb, neginf):
    dr = m.dr
    for hh in range(4):
        h = 4 * hg + hh
        dma(c, "pool", alb[0:1, :], dr["l3_alibir"][h:h + 1, :], K(alb))
        dma(c, "pool", alb[64:65, :], dr["l3_alibir"][h:h + 1, :], K(alb))
        emit_softmax_core(c, m, Q, KK, Vtok,
                          ctx=[ctxall[2 * hg], ctxall[2 * hg + 1]],
                          causal=True, alb=alb, albc=albc, mska=mska,
                          mskb=mskb, neginf=neginf, head_off=4 * hg,
                          single_head=hh)



def emit_l3_ffn(c, m):
    dr = m.dr
    PS = m.PS
    wst = c.scratch("p_wst0", [128, 1024], F32)
    cnw1 = [c.scratch(f"p_wres{i}", [128, 1024], BF16) for i in range(NCH)]
    cnw2 = [c.scratch(f"t3_w2_{i}", [128, 512], BF16) for i in range(NCH)]
    cnw3 = [c.scratch(f"t3_w3_{i}", [128, 256], BF16) for i in range(4)]
    cnw4 = [c.scratch(f"t3_w4_{i}", [128, 1], BF16) for i in range(2)]
    cols = c.scratch("t3_cols", [128, 16], F32)
    sc14 = c.scratch("t3_sc", [1, 8], F32)
    Oa = [c.scratch(f"t3_Oa{i}", [128, 512], F32) for i in range(NCH)]
    Ob = [c.scratch(f"t3_Ob{i}", [128, 512], F32) for i in range(NCH)]
    Ocb = [c.scratch(f"t3_Oc{i}", [128, 512], BF16) for i in range(NCH)]
    h1b = [c.scratch(f"t3_h1{i}", [128, 512], BF16) for i in range(NCH)]
    h2b = [c.scratch(f"t3_h2{i}", [128, 512], BF16) for i in range(4)]
    h3b = [c.scratch(f"t3_h3{i}", [128, 512], BF16) for i in range(2)]
    gcc = c.scratch("t3_gcc", [128, 16], F32)
    gccb = c.scratch("t3_gccb", [128, 16], BF16)
    tf = c.scratch("t3_tf", [1, 512], F32)
    cplp = c.scratch("sh_s0", [128, 512], F32)
    cpln = c.scratch("sh_s1", [128, 512], F32)
    twt = c.scratch("sh_s2", [128, 512], F32)

    for dc in range(NCH):
        red(c, gcc[:, dc:dc + 1], m.XL[dc][:], ALU.add, [K(m.XL[dc])], K(gcc))
    ts(c, gcc[:, 0:8], gcc[:, 0:8], f32(1.0 / T), ALU.mult, [K(gcc)], K(gcc))
    cp(c, gccb[:, 0:8], gcc[:, 0:8], [K(gcc)], K(gccb))
    for dc in range(NCH):
        dma(c, "sp", wst[:, 0:512], dr["l3_gcw1"][dc], K(wst))
        cp(c, cnw2[dc][:], wst[:, 0:512], [K(wst)], K(cnw2[dc]), eng="pool")
    for dc in range(4):
        dma(c, "sp", wst[:, 0:256], dr["l3_gcw2"][dc], K(wst))
        cp(c, cnw3[dc][:], wst[:, 0:256], [K(wst)], K(cnw3[dc]), eng="pool")
    for dc in range(2):
        dma(c, "sp", wst[:, 0:1], dr["l3_gcw3"][dc], K(wst))
        cp(c, cnw4[dc][:], wst[:, 0:1], [K(wst)], K(cnw4[dc]), eng="pool")
    dma(c, "sp", cols[:, 0:4], dr["l3_gb1"], K(cols))
    dma(c, "sp", cols[:, 4:6], dr["l3_gb2"], K(cols))
    dma(c, "sp", sc14[0:1, 0:1], dr["l3_gb3"], K(sc14))
    dma(c, "sp", sc14[0:1, 1:2], dr["l3_ar"], K(sc14))
    for oc in range(4):
        for dc in range(NCH):
            mm(c, PS[0][:, 0:1], cnw2[dc][:, 128 * oc:128 * (oc + 1)],
               gccb[:, dc:dc + 1], [K(cnw2[dc]), K(gccb)], K(PS[0]),
               dc == 0, dc == NCH - 1)
        act(c, gcc[:, 8 + oc:9 + oc], PS[0][:, 0:1], AF.Gelu,
            [K(PS[0]), K(cols)], K(gcc), bias=cols[:, oc:oc + 1])
    cp(c, gccb[:, 8:12], gcc[:, 8:12], [K(gcc)], K(gccb))
    for oc in range(2):
        for dc in range(4):
            mm(c, PS[0][:, 0:1], cnw3[dc][:, 128 * oc:128 * (oc + 1)],
               gccb[:, 8 + dc:9 + dc], [K(cnw3[dc]), K(gccb)], K(PS[0]),
               dc == 0, dc == 3)
        act(c, gcc[:, 12 + oc:13 + oc], PS[0][:, 0:1], AF.Gelu,
            [K(PS[0]), K(cols)], K(gcc), bias=cols[:, 4 + oc:5 + oc])
    cp(c, gccb[:, 12:14], gcc[:, 12:14], [K(gcc)], K(gccb))
    for dc in range(2):
        mm(c, PS[0][0:1, 0:1], cnw4[dc][:], gccb[:, 12 + dc:13 + dc],
           [K(cnw4[dc]), K(gccb)], K(PS[0]), dc == 0, dc == 1)
    act(c, sc14[0:1, 2:3], PS[0][0:1, 0:1], AF.Sigmoid, [K(PS[0]), K(sc14)],
        K(sc14), bias=sc14[0:1, 0:1])
    ts(c, sc14[0:1, 2:3], sc14[0:1, 2:3], f32(0.7), ALU.mult, [K(sc14)],
       K(sc14))

    for dc in range(NCH):
        dma(c, "sp", wst[:], dr["l3_cnw1"][dc], K(wst))
        cp(c, cnw1[dc][:], wst[:], [K(wst)], K(cnw1[dc]), eng="pool")
        dma(c, "sp", wst[:, 0:512], dr["l3_cnw2"][dc], K(wst))
        cp(c, cnw2[dc][:], wst[:, 0:512], [K(wst)], K(cnw2[dc]), eng="pool")
    for dc in range(4):
        dma(c, "sp", wst[:, 0:256], dr["l3_cnw3"][dc], K(wst))
        cp(c, cnw3[dc][:], wst[:, 0:256], [K(wst)], K(cnw3[dc]), eng="pool")
    for dc in range(2):
        dma(c, "sp", wst[:, 0:1], dr["l3_cnw4"][dc], K(wst))
        cp(c, cnw4[dc][:], wst[:, 0:1], [K(wst)], K(cnw4[dc]), eng="pool")
    dma(c, "sp", cols[:, 0:8], dr["l3_b1"], K(cols))
    dma(c, "sp", cols[:, 8:12], dr["l3_b2"], K(cols))
    dma(c, "sp", cols[:, 12:14], dr["l3_b3"], K(cols))
    dma(c, "sp", sc14[0:1, 3:4], dr["l3_b4"], K(sc14))

    perm = [4, 5, 6, 7, 0, 1, 2, 3]
    sign = [1, 1, -1, -1, -1, -1, 1, 1]
    for tb in range(4):
        tsl = slice(512 * tb, 512 * (tb + 1))
        for dc in range(NCH):
            cp(c, Oa[dc][:], m.XL[dc][:, tsl], [K(m.XL[dc])], K(Oa[dc]))
        Ocur, Onxt = Oa, Ob
        for cyc in range(3):
            for dc in range(NCH):
                cp(c, Ocb[dc][:], Ocur[dc][:], [K(Ocur[dc])], K(Ocb[dc]))
            for oc in range(NCH):
                for dc in range(NCH):
                    mm(c, PS[oc % 2][:], cnw1[dc][:, 128 * oc:128 * (oc + 1)],
                       Ocb[dc][:], [K(cnw1[dc]), K(Ocb[dc])], K(PS[oc % 2]),
                       dc == 0, dc == NCH - 1)
                act(c, h1b[oc][:], PS[oc % 2][:], AF.Gelu,
                    [K(PS[oc % 2]), K(cols)], K(h1b[oc]),
                    bias=cols[:, oc:oc + 1])
            for oc in range(4):
                for dc in range(NCH):
                    mm(c, PS[oc % 2][:], cnw2[dc][:, 128 * oc:128 * (oc + 1)],
                       h1b[dc][:], [K(cnw2[dc]), K(h1b[dc])], K(PS[oc % 2]),
                       dc == 0, dc == NCH - 1)
                act(c, h2b[oc][:], PS[oc % 2][:], AF.Gelu,
                    [K(PS[oc % 2]), K(cols)], K(h2b[oc]),
                    bias=cols[:, 8 + oc:9 + oc])
            for oc in range(2):
                for dc in range(4):
                    mm(c, PS[oc % 2][:], cnw3[dc][:, 128 * oc:128 * (oc + 1)],
                       h2b[dc][:], [K(cnw3[dc]), K(h2b[dc])], K(PS[oc % 2]),
                       dc == 0, dc == 3)
                act(c, h3b[oc][:], PS[oc % 2][:], AF.Gelu,
                    [K(PS[oc % 2]), K(cols)], K(h3b[oc]),
                    bias=cols[:, 12 + oc:13 + oc])
            for dc in range(2):
                mm(c, PS[2][0:1, :], cnw4[dc][:], h3b[dc][:],
                   [K(cnw4[dc]), K(h3b[dc])], K(PS[2]), dc == 0, dc == 1)
            act(c, tf[:], PS[2][0:1, :], AF.Sigmoid, [K(PS[2]), K(sc14)],
                K(tf), bias=sc14[0:1, 3:4])
            ts(c, tf[:], tf[:], f32(0.3), ALU.mult, [K(tf)], K(tf))
            ts(c, tf[:], tf[:], sc14[0:1, 2:3], ALU.add, [K(tf), K(sc14)],
               K(tf))
            ts(c, tf[:], tf[:], f32(-0.5), ALU.add, [K(tf)], K(tf))
            ts(c, tf[:], tf[:], f32(2.0), ALU.mult, [K(tf)], K(tf))
            ts(c, tf[:], tf[:], sc14[0:1, 1:2], ALU.mult, [K(tf), K(sc14)],
               K(tf))
            ts(c, tf[:], tf[:], f32(0.1), ALU.add, [K(tf)], K(tf))
            mm(c, PS[2][:], m.ones_f[0:1, :], tf[:], [K(m.ones_f), K(tf)],
               K(PS[2]), True, True)
            cp(c, cplp[:], PS[2][:], [K(PS[2])], K(cplp))
            ts(c, cpln[:], cplp[:], f32(-1.0), ALU.mult, [K(cplp)], K(cpln))
            for dc in range(NCH):
                cb = cplp if sign[dc] > 0 else cpln
                tt(c, twt[:], Ocur[perm[dc]][:], cb[:], ALU.mult,
                   [K(Ocur[perm[dc]]), K(cb)], K(twt))
                tt(c, Onxt[dc][:], Ocur[dc][:], twt[:], ALU.add,
                   [K(Ocur[dc]), K(twt)], K(Onxt[dc]))
            Ocur, Onxt = Onxt, Ocur
        for dc in range(NCH):
            stt(c, m.R[dc][:, tsl], Ocur[dc][:], f32(0.5), m.R[dc][:, tsl],
                ALU.mult, ALU.add, [K(Ocur[dc]), K(m.R[dc])], K(m.R[dc]))



def host_consts():
    d = {}
    d["c_ones_f"] = np.ones((128, 128), np.float32)
    d["c_ones_b"] = np.ones((128, 128), BF)
    d["c_ident_b"] = np.eye(128, dtype=BF)
    d["c_ident_f"] = np.eye(128, dtype=np.float32)
    p = np.arange(128)
    cidx = np.arange(1536)
    d["c_stair"] = ((cidx[None, :] - 384) >= p[:, None]).astype(BF)
    q = np.arange(128)
    d["c_trilm"] = (q[None, :] <= p[:, None]).astype(BF)
    selh = np.zeros((8, 1024), np.float32)
    for h in range(8):
        selh[h, 128 * h:128 * (h + 1)] = 1.0
    d["c_sel"] = selh
    return d


def host_stage(inputs):
    s = dict(host_consts())
    w12 = np.asarray(inputs["L0_ffn_w12"])
    s["l0_w12T"] = np.ascontiguousarray(
        w12.reshape(8, 32, 128, 8, 128).transpose(0, 1, 4, 3, 2)
    ).reshape(8, 32, 128, 1024)
    w3 = np.asarray(inputs["L0_ffn_w3"])
    s["l0_w3T"] = np.ascontiguousarray(
        w3.reshape(8, 8, 128, 16, 128).transpose(0, 3, 4, 1, 2)
    ).reshape(8, 16, 128, 1024)
    gw = np.asarray(inputs["L0_ffn_gate_w"])
    s["l0_gateT"] = np.ascontiguousarray(
        gw.reshape(8, 8, 128).transpose(2, 1, 0)).reshape(128, 64)
    for nm, wn in (("l0_WkT", "L0_attn_Wk"), ("l0_WqT", "L0_attn_Wq"),
                   ("l0_WvT", "L0_attn_Wv")):
        W = np.asarray(inputs[wn])
        s[nm] = np.ascontiguousarray(W.T).reshape(8, 128, 512)
    Wb = np.asarray(inputs["L0_attn_Wb"])
    s["l0_WbT"] = np.ascontiguousarray(
        Wb.T.reshape(8, 128, 8).transpose(1, 0, 2)).reshape(128, 64)
    Wo = np.asarray(inputs["L0_attn_Wo"])
    s["l0_WoT2"] = np.ascontiguousarray(
        Wo.reshape(8, 128, 4, 128).transpose(0, 3, 2, 1)).reshape(8, 128, 512)
    for nm, wn in (("l1_wq", "L1_attn_wq"), ("l1_wk", "L1_attn_wk"),
                   ("l1_wv", "L1_attn_wv"), ("l1_wo", "L1_attn_wo")):
        s[nm] = np.ascontiguousarray(np.asarray(inputs[wn]).reshape(8, 128, 1024))
    for nm, wn in (("l1_bq", "L1_attn_bq"), ("l1_bk", "L1_attn_bk"),
                   ("l1_bv", "L1_attn_bv"), ("l1_bo", "L1_attn_bo")):
        s[nm] = np.ascontiguousarray(np.asarray(inputs[wn]).reshape(8, 128).T)
    w1 = np.asarray(inputs["L1_ffn_w1"])
    s["l1_w1T"] = np.ascontiguousarray(
        w1.reshape(8, 8, 128, 32, 128).transpose(0, 3, 2, 1, 4)
    ).reshape(8, 32, 128, 1024)
    s["l1_w2"] = np.ascontiguousarray(
        np.asarray(inputs["L1_ffn_w2"]).reshape(8, 32, 128, 1024))
    s["l1_bias"] = np.ascontiguousarray(
        np.asarray(inputs["L1_ffn_bias"]).reshape(8, 128).T)
    s["l2_Wq"] = np.ascontiguousarray(
        np.asarray(inputs["L2_ffn_Wq"]).reshape(8, 128, 1024))
    s["l2_Wo"] = np.ascontiguousarray(
        np.asarray(inputs["L2_ffn_Wo"]).reshape(8, 128, 1024))
    s["l2_Wdn"] = np.ascontiguousarray(
        np.asarray(inputs["L2_ffn_Wdown"]).reshape(8, 128, 64)
        .transpose(1, 0, 2)).reshape(128, 512)
    s["l2_Wg"] = np.ascontiguousarray(
        np.asarray(inputs["L2_ffn_Wg"]).reshape(8, 128, 8)
        .transpose(1, 0, 2)).reshape(128, 64)
    s["l2_Wup"] = np.ascontiguousarray(np.asarray(inputs["L2_ffn_Wup"]))
    s["l2_We"] = np.ascontiguousarray(
        np.asarray(inputs["L2_ffn_We"]).reshape(8, 8, 128, 1024))
    for nm, wn in (("l2_bq", "L2_ffn_bq"), ("l2_bo", "L2_ffn_bo"),
                   ("l2_g1", "L2_ffn_gamma1"), ("l2_g2", "L2_ffn_gamma2")):
        s[nm] = np.ascontiguousarray(np.asarray(inputs[wn]).reshape(8, 128).T)
    s["l2_bdn"] = np.ascontiguousarray(
        np.asarray(inputs["L2_ffn_bdown"]).reshape(64, 1))
    s["l2_bup"] = np.ascontiguousarray(
        np.asarray(inputs["L2_ffn_bup"]).reshape(16, 128).T)
    s["l2_bgB"] = np.ascontiguousarray(np.broadcast_to(
        np.asarray(inputs["L2_ffn_bg"])[None, :], (128, 8)).copy())
    wqkv = np.asarray(inputs["L2_attn_w_qkv"])
    blk = np.zeros((8, 1024, 640), np.float32)
    for g in range(8):
        for cab in range(2):
            for hh in range(2):
                h = 2 * g + hh
                rows = wqkv[h * 64 + cab * 32:h * 64 + cab * 32 + 32, :]
                blk[g, :, cab * 128 + hh * 64:cab * 128 + hh * 64 + 32] = rows.T
                rowk = wqkv[1024 + h * 64 + cab * 32:1024 + h * 64 + cab * 32 + 32, :]
                blk[g, :, 256 + cab * 128 + hh * 64:256 + cab * 128 + hh * 64 + 32] = rowk.T
        for hh in range(2):
            h = 2 * g + hh
            rowv = wqkv[2048 + h * 64:2048 + h * 64 + 64, :]
            blk[g, :, 512 + hh * 64:512 + hh * 64 + 64] = rowv.T
    s["l2a_wqkvT"] = np.ascontiguousarray(
        blk.reshape(8, 8, 128, 640))
    s["l2a_wprojT"] = np.ascontiguousarray(
        np.asarray(inputs["L2_attn_w_proj"]).T.reshape(8, 128, 1024))
    s["l2a_bproj"] = np.ascontiguousarray(
        np.asarray(inputs["L2_attn_b_proj"]).reshape(8, 128).T)
    for nm, wn in (("l2a_lq1", "L2_attn_lambda_q1"),
                   ("l2a_lk1", "L2_attn_lambda_k1"),
                   ("l2a_lq2", "L2_attn_lambda_q2"),
                   ("l2a_lk2", "L2_attn_lambda_k2")):
        s[nm] = np.ascontiguousarray(np.asarray(inputs[wn]).reshape(1, 32))
    s["l2a_snw"] = np.ascontiguousarray(
        np.tile(np.asarray(inputs["L2_attn_sub_norm_w"]), 2).reshape(128, 1))
    s["l3_wqkv"] = np.ascontiguousarray(
        np.asarray(inputs["L3_attn_w_qkv"]).reshape(8, 128, 3072))
    s["l3_wproj"] = np.ascontiguousarray(
        np.asarray(inputs["L3_attn_w_proj"]).reshape(8, 128, 1024))
    import math as _math

    def _slopes(n):
        def p2(n):
            start = 2 ** (-2 ** (-(_math.log2(n) - 3)))
            return [start * start ** i for i in range(n)]
        if _math.log2(n).is_integer():
            return p2(n)
        cc = 2 ** _math.floor(_math.log2(n))
        return p2(cc) + _slopes(2 * cc)[0::2][:n - cc]
    sl = np.asarray(_slopes(16), np.float32)
    ar = np.arange(T, dtype=np.float32)
    s["l3_alibir"] = np.ascontiguousarray((sl[:, None] * ar[None, :]).astype(BF))
    p = np.arange(128)
    albc = np.zeros((128, 256), np.float32)
    for h in range(16):
        for j in range(16):
            albc[:, 16 * h + j] = sl[h] * (128 * j + p)
    s["l3_alibic"] = albc
    q = np.arange(128)
    s["l3_mska"] = np.where(q[None, :] <= p[:, None], 0.0, -1e30).astype(np.float32)
    s["l3_mskb"] = np.where(q[None, :] >= p[:, None], 0.0, -1e30).astype(np.float32)
    s["l3_neginf"] = np.full((128, 128), -1e30, np.float32)
    s["l3_cnw1"] = np.ascontiguousarray(
        np.asarray(inputs["L3_ffn_cn_w1"]).reshape(8, 128, 1024))
    s["l3_cnw2"] = np.ascontiguousarray(
        np.asarray(inputs["L3_ffn_cn_w2"]).reshape(8, 128, 512))
    s["l3_cnw3"] = np.ascontiguousarray(
        np.asarray(inputs["L3_ffn_cn_w3"]).reshape(4, 128, 256))
    s["l3_cnw4"] = np.ascontiguousarray(
        np.asarray(inputs["L3_ffn_cn_w4"]).reshape(2, 128, 1))
    s["l3_gcw1"] = np.ascontiguousarray(
        np.asarray(inputs["L3_ffn_gc_w1"]).reshape(8, 128, 512))
    s["l3_gcw2"] = np.ascontiguousarray(
        np.asarray(inputs["L3_ffn_gc_w2"]).reshape(4, 128, 256))
    s["l3_gcw3"] = np.ascontiguousarray(
        np.asarray(inputs["L3_ffn_gc_w3"]).reshape(2, 128, 1))
    s["l3_b1"] = np.ascontiguousarray(
        np.asarray(inputs["L3_ffn_cn_b1"]).reshape(8, 128).T)
    s["l3_b2"] = np.ascontiguousarray(
        np.asarray(inputs["L3_ffn_cn_b2"]).reshape(4, 128).T)
    s["l3_b3"] = np.ascontiguousarray(
        np.asarray(inputs["L3_ffn_cn_b3"]).reshape(2, 128).T)
    s["l3_b4"] = np.ascontiguousarray(
        np.asarray(inputs["L3_ffn_cn_b4"]).reshape(1, 1))
    s["l3_gb1"] = np.ascontiguousarray(
        np.asarray(inputs["L3_ffn_gc_b1"]).reshape(4, 128).T)
    s["l3_gb2"] = np.ascontiguousarray(
        np.asarray(inputs["L3_ffn_gc_b2"]).reshape(2, 128).T)
    s["l3_gb3"] = np.ascontiguousarray(
        np.asarray(inputs["L3_ffn_gc_b3"]).reshape(1, 1))
    s["l3_ar"] = np.ascontiguousarray(
        np.asarray(inputs["L3_ffn_adaptive_range"]).reshape(1, 1))
    return s


def host_l1_routing(inputs):
    ei = np.asarray(inputs["L1_ffn_expert_indices"]).astype(np.float32)
    ew = np.asarray(inputs["L1_ffn_expert_weights"]).astype(np.float32)
    out = []
    for core in range(8):
        sl = slice(512 * core, 512 * (core + 1))
        ic = ei[sl].reshape(4, 128, 2).transpose(1, 0, 2).reshape(128, 8)
        wc = ew[sl].reshape(4, 128, 2).transpose(1, 0, 2).reshape(128, 8)
        out.append((np.ascontiguousarray(ic), np.ascontiguousarray(wc)))
    return out



class LazyDr(dict):
    def __init__(self, core, staged):
        super().__init__()
        self._core = core
        self._staged = staged

    def __missing__(self, name):
        ap = self._core.dram_in(name, self._staged[name])
        self[name] = ap
        return ap


def build_program(subls, staged, per_core_names=()):
    c = Core()
    dr = LazyDr(c, staged)
    dr["x_in"] = c.dram_in("x_in", np.zeros((D, T), np.float32))
    for nm, shape, dt in per_core_names:
        dr[nm] = c.dram_in(nm, np.zeros(shape, dt))
    out_ap = c.dram_out("x_out", [D, T], F32)
    mod = Mod(c, dr)
    emit_load_x(c, mod)
    for sl_name in subls:
        emit_ln(c, mod)
        if sl_name in ("l0f", "l0f2"):
            emit_l0_ffn(c, mod, pf=sl_name)
        elif sl_name == "l0a":
            emit_l0_attn(c, mod)
        elif sl_name == "l1a":
            emit_l1_attn(c, mod)
        elif sl_name in ("l1f", "l1f2"):
            emit_l1_ffn(c, mod)
        elif sl_name in ("l2f", "l2f2"):
            emit_l2_ffn(c, mod)
        elif sl_name == "l2a":
            emit_l2_attn(c, mod)
        elif sl_name == "l3a":
            emit_l3_attn(c, mod)
        elif sl_name in ("l3f", "l3f2"):
            emit_l3_ffn(c, mod)
        else:
            raise NotImplementedError(sl_name)
    emit_store_x(c, mod, out_ap)
    c.p.finalize(c.es)
    c.es.close()
    return c


def _to_l1_order(xT):
    xall = np.concatenate(xT, axis=1)
    out = []
    for corei in range(8):
        ex = xall[:, 512 * corei:512 * (corei + 1)]
        bi = xall[:, 4096 + 1536 * corei:4096 + 1536 * (corei + 1)]
        out.append(np.ascontiguousarray(np.concatenate([ex, bi], axis=1)))
    return out


def _from_l1_order(yT):
    cols = [None] * 2
    ex = np.concatenate([y[:, 0:512] for y in yT], axis=1)
    bi = np.concatenate([y[:, 512:2048] for y in yT], axis=1)
    xall = np.concatenate([ex, bi], axis=1)
    return [np.ascontiguousarray(xall[:, 2048 * b:2048 * (b + 1)])
            for b in range(8)]


def run_chain(x, staged, launch_groups, n_cores=8, routing=None):
    xT = [np.ascontiguousarray(x[b].T).astype(np.float32) for b in range(8)]
    for subls in launch_groups:
        is_l1f = subls in (["l1f"], ["l1f2"])
        pcn = [("l1_idxf", (128, 8), np.float32),
               ("l1_ewf", (128, 8), np.float32)] if is_l1f else []
        c = build_program(subls, staged, per_core_names=pcn)
        shared = {k: v for k, v in c.in_arrays.items()
                  if k not in ("x_in", "l1_idxf", "l1_ewf")}
        if is_l1f:
            xT_in = _to_l1_order(xT)
            in_maps = [dict(shared, x_in=xT_in[b], l1_idxf=routing[b][0],
                            l1_ewf=routing[b][1]) for b in range(n_cores)]
        else:
            xT_in = xT
            in_maps = [dict(shared, x_in=xT_in[b]) for b in range(n_cores)]
        res = run_bass_kernel_spmd(c.nc, in_maps, core_ids=list(range(n_cores)))
        yT = [np.asarray(res.results[b]["x_out"]) for b in range(n_cores)]
        xT = _from_l1_order(yT) if is_l1f else yT
    return np.stack([t.T for t in xT], axis=0)


def kernel(**inputs):
    staged = host_stage(inputs)
    routing = host_l1_routing(inputs)
    x = np.asarray(inputs["L0_attn_x"], np.float32)
    groups = [[s] for s in ALL_SUBLAYERS]
    return run_chain(x, staged, groups, routing=routing)
